# version 04
# baseline (speedup 1.0000x reference)
import numpy as np
import ml_dtypes
import concourse.bass as bass
import concourse.mybir as mybir
from concourse.bass_utils import run_bass_kernel_spmd

F32 = mybir.dt.float32
BF16 = mybir.dt.bfloat16
AF = mybir.ActivationFunctionType
ALU = mybir.AluOpType

NCORE = 8
NB = 8
D = 1024
H = 16
DH = 64
CTOK = 512
PTOK = 256
REG = 544
XNW = NB * REG + 112
NCOL = NB * CTOK

KH0, VB0, QH0, PB0, SLAB0 = 0, 6144, 12384, 16480, 19808

UPTO = 5



def _host_prep(inputs):
    x = np.ascontiguousarray(np.asarray(inputs["L0_attn_x"], dtype=np.float32))
    g = lambda n: np.asarray(inputs[n], dtype=np.float32)
    w_qkv, b_qkv = g("L0_attn_w_qkv"), g("L0_attn_b_qkv")
    w_out, b_out = g("L0_attn_w_out"), g("L0_attn_b_out")
    wf, uf, bf_ = g("L0_ffn_wf"), g("L0_ffn_uf"), g("L0_ffn_bf")
    wi, ui, bi = g("L0_ffn_wi"), g("L0_ffn_ui"), g("L0_ffn_bi")
    wc, uc, bc = g("L0_ffn_wc"), g("L0_ffn_uc"), g("L0_ffn_bc")
    wo, uo, bo = g("L0_ffn_wo"), g("L0_ffn_uo"), g("L0_ffn_bo")
    in_c = g("L0_ffn_in_c")

    WcatT = np.empty((D, 3 * D), np.float32)
    UcatT = np.empty((D, 3 * D), np.float32)
    gW, gU, gB = (wf, wi, wc), (uf, ui, uc), (bf_, bi, bc)
    gbias = np.empty((128, 24), np.float32)
    for oc in range(24):
        j, gg = oc // 3, oc % 3
        WcatT[:, oc * 128:(oc + 1) * 128] = gW[gg][j * 128:(j + 1) * 128].T
        UcatT[:, oc * 128:(oc + 1) * 128] = gU[gg][j * 128:(j + 1) * 128].T
        gbias[:, oc] = gB[gg][j * 128:(j + 1) * 128]
    wqkvT = np.ascontiguousarray(w_qkv.T)
    woutT = np.ascontiguousarray(w_out.T)
    woT = np.ascontiguousarray(wo.T)
    uoT = np.ascontiguousarray(uo.T)

    cols128 = lambda v: np.ascontiguousarray(v.reshape(8, 128).T)
    bias_all = np.zeros((128, 72), np.float32)
    bias_all[:, 0:8] = cols128(b_qkv[0:D])
    bias_all[:, 8:16] = cols128(b_qkv[D:2 * D])
    bias_all[:, 16:24] = cols128(b_qkv[2 * D:])
    bias_all[:, 24:48] = gbias
    bias_all[:, 48:56] = cols128(bo)
    bias_all[:, 56:64] = cols128(b_out)
    bias_all[:, 64:72] = cols128(in_c)
    bvrow = np.ascontiguousarray(b_qkv[2 * D:].reshape(1, D))

    sel = np.zeros((64, 1024), ml_dtypes.bfloat16)
    for j in range(8):
        sel[2 * j, j * 128: j * 128 + 64] = 1
        sel[2 * j + 1, j * 128 + 64: j * 128 + 128] = 1
    sel2 = np.zeros((128, 128), ml_dtypes.bfloat16)
    for j in range(8):
        sel2[0:64, j * 16 + 2 * j] = 1
        sel2[64:128, j * 16 + 2 * j + 1] = 1
    ones = np.ones((128, 128), ml_dtypes.bfloat16)

    per_core = []
    for c in range(NCORE):
        base = c * CTOK
        xe = np.zeros((NB, REG, D), np.float32)
        for p in range(2):
            raws = base - 16 + p + 2 * np.arange(8)
            ok = (raws >= 0) & (raws < 4096)
            xe[:, p * 272 + np.arange(8)[ok]] = x[:, raws[ok]]
            xe[:, p * 272 + 8: p * 272 + 264] = x[:, base + p + 2 * np.arange(PTOK)]
            raws = base + CTOK + p + 2 * np.arange(8)
            ok = (raws >= 0) & (raws < 4096)
            xe[:, p * 272 + 264 + np.arange(8)[ok]] = x[:, raws[ok]]
        xT = np.ascontiguousarray(xe.reshape(NB * REG, D).T)
        xr = np.ascontiguousarray(xe[:, np.r_[8:264, 280:536], :].reshape(NB * CTOK, D).T)
        masks = np.zeros((128, 512), ml_dtypes.bfloat16)
        for qb in range(2):
            for ab in range(2):
                t = qb + ab
                for r in range(128):
                    gpos = 256 * c - 8 + 128 * t + r
                    if not (0 <= gpos < 2048):
                        continue
                    d0 = (128 * t + r) - (8 + 128 * qb)
                    qlo, qhi = max(0, d0 - 8), min(128, d0 + 9)
                    if qlo < qhi:
                        masks[r, (qb * 2 + ab) * 128 + qlo:(qb * 2 + ab) * 128 + qhi] = 1
        per_core.append({
            "xT": xT, "xr": xr, "masks": np.ascontiguousarray(masks),
            "wqkvT": wqkvT, "woutT": woutT, "WcatT": WcatT, "UcatT": UcatT,
            "woT": woT, "uoT": uoT, "bias_all": bias_all, "bvrow": bvrow,
            "sel": sel, "sel2": sel2, "ones": ones,
        })
    return per_core


def _host_assemble(outs):
    full = np.empty((NB, 4096, D), np.float32)
    for c in range(NCORE):
        t = outs[c].reshape(D, NB, 2, PTOK).transpose(1, 3, 2, 0)
        full[:, c * CTOK:(c + 1) * CTOK, :] = t.reshape(NB, CTOK, D)
    return full



class _S:
    def __init__(self):
        self.q = {"tensor": [], "vector": [], "scalar": [], "gpsimd": [], "sync": []}
        self.cnt = {}

    def emit(self, eng, fn):
        self.q[eng].append(fn)

    def wait(self, eng, sem, val):
        if val > 0:
            self.emit(eng, lambda e, s=sem, v=val: e.wait_ge(s, v))

    def bump(self, sem, n=1):
        k = id(sem)
        c = self.cnt.get(k, 0) + n
        self.cnt[k] = c
        return c

    def val(self, sem):
        return self.cnt.get(id(sem), 0)

    def mark(self, eng, sem, n=1):
        last = self.q[eng].pop()
        self.q[eng].append(lambda e, f=last, s=sem, k=n: f(e).then_inc(s, k))
        return self.bump(sem, n)



def _build(taps=(), upto=UPTO, ffn_steps=NB, p3_b=NB, p3_parts=8):
    nc = bass.Bass("TRN2", target_bir_lowering=False, debug=False)
    din = {}
    for name, shape, dt in [
        ("xT", [D, NB * REG], F32), ("xr", [D, NCOL], F32),
        ("wqkvT", [D, 3 * D], F32), ("woutT", [D, D], F32),
        ("WcatT", [D, 3 * D], F32), ("UcatT", [D, 3 * D], F32),
        ("woT", [D, D], F32), ("uoT", [D, D], F32),
        ("bias_all", [128, 72], F32), ("bvrow", [1, D], F32),
        ("sel", [64, D], BF16), ("sel2", [128, 128], BF16),
        ("ones", [128, 128], BF16), ("masks", [128, 512], BF16),
    ]:
        din[name] = nc.dram_tensor(name, shape, dt, kind="ExternalInput").ap()
    out_h = nc.dram_tensor("out", [D, NCOL], F32, kind="ExternalOutput").ap()
    tap_h = {}
    for tname, tshape, tdt in taps:
        tap_h[tname] = nc.dram_tensor("tap_" + tname, tshape, tdt, kind="ExternalOutput").ap()

    import contextlib
    with contextlib.ExitStack() as stk:
        E = stk.enter_context
        XN = E(nc.sbuf_tensor("XN", [128, 8 * XNW], BF16))
        W1 = E(nc.sbuf_tensor("W1", [128, 24576], BF16))
        W2 = E(nc.sbuf_tensor("W2", [128, 4096], BF16))
        CB = E(nc.sbuf_tensor("CB", [128, NCOL], BF16))
        CB1 = E(nc.sbuf_tensor("CB1", [128, NCOL], BF16))
        CF = E(nc.sbuf_tensor("CF", [128, NCOL], F32))
        CUO = E(nc.sbuf_tensor("CUO", [128, NCOL], BF16))
        LNB = E(nc.sbuf_tensor("LNB", [128, 2 * 1088], BF16))
        FS = E(nc.sbuf_tensor("FS", [128, 3072], F32))
        BIA = E(nc.sbuf_tensor("BIA", [128, 72], F32))
        MSK = E(nc.sbuf_tensor("MSK", [128, 512], BF16))
        SEL = E(nc.sbuf_tensor("SEL", [64, D], BF16))
        SEL2 = E(nc.sbuf_tensor("SEL2", [128, 128], BF16))
        ONE = E(nc.sbuf_tensor("ONE", [128, 128], BF16))
        BVB = E(nc.sbuf_tensor("BVB", [1, D], BF16))
        ROWS = E(nc.sbuf_tensor("ROWS", [1, 1824], F32))
        ROWB = E(nc.sbuf_tensor("ROWB", [1, 1024], BF16))
        RN = E(nc.sbuf_tensor("RN", [16, 512], F32))
        RNB = E(nc.sbuf_tensor("RNB", [64, 512], BF16))
        ZR = E(nc.sbuf_tensor("ZR", [1, 1024], BF16))
        RZS = E(nc.sbuf_tensor("RZS", [64, 1024], BF16))
        GS = E(nc.sbuf_tensor("GS", [128, 1792], BF16))
        MRK = E(nc.sbuf_tensor("MRK", [128, 8], F32))
        pk = [E(nc.psum_tensor("pk%d" % i, [128, 512], F32)) for i in range(8)]
        sT = E(nc.semaphore("sT"))
        sV = E(nc.semaphore("sV"))
        sS = E(nc.semaphore("sS"))
        sLD = E(nc.semaphore("sLD"))
        sXR = E(nc.semaphore("sXR"))
        sDO = E(nc.semaphore("sDO"))

        S = _S()
        bank_wr = [(sT, 0)] * 8
        bank_rd = [(sT, 0)] * 8

        def t_bank(i):
            sem, v = bank_rd[i]
            S.wait("tensor", sem, v)

        def r_bank(eng, i):
            sem, v = bank_wr[i]
            S.wait(eng, sem, v)

        def tmark():
            return S.mark("tensor", sT)

        def ld(dst, src):
            S.emit("sync", lambda e, d=dst, s=src: e.dma_start(out=d, in_=s).then_inc(sXR, 16))
            return S.bump(sXR, 16)

        ld(BIA[:, 0:72], din["bias_all"])
        ld(MSK[:, 0:512], din["masks"])
        ld(SEL[0:64, 0:D], din["sel"])
        ld(SEL2[:, 0:128], din["sel2"])
        consts_done = ld(ONE[:, 0:128], din["ones"])
        S.emit("gpsimd", lambda e: e.dma_start(out=BVB[0:1, 0:D], in_=din["bvrow"]).then_inc(sLD, 16))
        bvb_done = S.bump(sLD, 16)
        for ft in range(8):
            S.emit("vector", lambda e, f=ft: e.memset(XN[:, f * XNW + NB * REG:(f + 1) * XNW], 0.0))
        S.emit("vector", lambda e: e.memset(GS[:, 0:1792], 0.0))
        S.emit("vector", lambda e: e.memset(MRK[:, 0:8], 0.0))
        S.emit("vector", lambda e: e.memset(RNB[0:64, 0:512], 0.0))
        S.emit("vector", lambda e: e.memset(MRK[:, 2:3], 1e-5))
        S.emit("vector", lambda e: e.memset(MRK[:, 3:4], 1e-30))
        pad_done = S.mark("vector", sV)

        for k in range(8):
            S.emit("gpsimd", lambda e, f=k: e.dma_start(
                out=W1[:, f * 3072:(f + 1) * 3072],
                in_=din["WcatT"][f * 128:(f + 1) * 128, :]).then_inc(sLD, 16))
            S.bump(sLD, 16)
        wcat_done = S.val(sLD)

        xb_free = [(sV, 0), (sV, 0)]
        norm_marks = []

        def emit_ln_chunk(ci):
            ch = ci % 2
            bk6 = 6 + ch
            xsq = GS[:, 0:1088] if ch == 0 else CUO[:, 0:1088]
            ro = ch * 136
            off = (ci // 4) * REG + (ci % 4) * 136
            slot = ci % 2
            xb0 = 1088 * slot
            sem, v = xb_free[slot]
            S.wait("gpsimd", sem, v)
            src = din["xT"][:, off: off + 136]
            src = src.rearrange("(ft p) c -> p ft c", p=128)
            S.emit("gpsimd", lambda e, s=src, o=xb0:
                   e.dma_start(out=LNB[:, o:o + 1088].rearrange("p (ft c) -> p ft c", c=136),
                               in_=s).then_inc(sLD, 16))
            ld_done = S.bump(sLD, 16)

            S.wait("vector", sLD, ld_done)
            S.emit("vector", lambda e, o=xb0, q=xsq: e.tensor_mul(
                out=q, in0=LNB[:, o:o + 1088], in1=LNB[:, o:o + 1088]).then_inc(sV, 1))
            xsq_done = S.bump(sV)

            t_bank(bk6)
            S.wait("tensor", sV, xsq_done)
            if ci == 0:
                S.wait("tensor", sXR, consts_done)
            for ft in range(8):
                S.emit("tensor", lambda e, f=ft, o=xb0, b2=bk6: e.matmul(
                    pk[b2][0:1, 0:136], ONE[:, 0:1], LNB[:, o + f * 136: o + (f + 1) * 136],
                    start=(f == 0), stop=False))
                S.emit("tensor", lambda e, f=ft, b2=bk6, q=xsq: e.matmul(
                    pk[b2][0:1, 136:272], ONE[:, 0:1], q[:, f * 136:(f + 1) * 136],
                    start=False, stop=(f == 7)))
            stats_done = tmark()
            bank_wr[bk6] = (sT, stats_done)

            S.wait("vector", sT, stats_done)
            S.emit("vector", lambda e, b2=bk6, r=ro: e.tensor_scalar_mul(
                out=ROWS[0:1, r:r + 136], in0=pk[b2][0:1, 0:136], scalar1=1.0 / D))
            S.emit("vector", lambda e, r=ro: e.tensor_mul(
                out=ROWS[0:1, 512 + r:648 + r], in0=ROWS[0:1, r:r + 136], in1=ROWS[0:1, r:r + 136]))
            S.emit("vector", lambda e, b2=bk6, r=ro: e.scalar_tensor_tensor(
                out=ROWS[0:1, 1024 + r:1160 + r], in0=pk[b2][0:1, 136:272], scalar=1.0 / D,
                in1=ROWS[0:1, 512 + r:648 + r], op0=ALU.mult, op1=ALU.subtract))
            S.emit("vector", lambda e, r=ro: e.tensor_copy(
                out=ROWB[0:1, r:r + 136], in_=ROWS[0:1, r:r + 136]).then_inc(sV, 1))
            rows_done = S.bump(sV)

            S.wait("scalar", sV, rows_done)
            S.emit("scalar", lambda e, r=ro: e.activation(
                out=ROWS[0:1, 1536 + r:1672 + r], in_=ROWS[0:1, 1024 + r:1160 + r],
                func=AF.Ln, bias=MRK[0:1, 2:3]))
            S.emit("scalar", lambda e, r=ro: e.activation(
                out=ROWS[0:1, 1024 + r:1160 + r], in_=ROWS[0:1, 1536 + r:1672 + r],
                func=AF.Exp, scale=-0.5).then_inc(sS, 1))
            rstd_done = S.bump(sS)
            S.wait("vector", sS, rstd_done)
            S.emit("vector", lambda e, r=ro: e.tensor_copy(
                out=ROWB[0:1, 512 + r:648 + r], in_=ROWS[0:1, 1024 + r:1160 + r]).then_inc(sV, 1))
            rowb_done = S.bump(sV)

            bank_rd[bk6] = (sV, rows_done)
            t_bank(bk6)
            S.wait("tensor", sV, rowb_done)
            S.emit("tensor", lambda e, b2=bk6, r=ro: e.matmul(
                pk[b2][:, 0:136], ONE[0:1, 0:128], ROWB[0:1, r:r + 136], start=True, stop=False))
            S.emit("tensor", lambda e, b2=bk6, r=ro: e.matmul(
                pk[b2][:, 136:272], ONE[0:1, 0:128], ROWB[0:1, 512 + r:648 + r], start=False, stop=True))
            bc_done = tmark()
            bank_wr[bk6] = (sT, bc_done)

            S.wait("vector", sT, bc_done)
            if ci == 0:
                S.wait("vector", sV, pad_done)
            for ft in range(8):
                dst = XN[:, ft * XNW + off: ft * XNW + off + 136]
                S.emit("vector", lambda e, f=ft, o=xb0, b2=bk6: e.tensor_sub(
                    out=GS[:, 1088:1224], in0=LNB[:, o + f * 136: o + (f + 1) * 136],
                    in1=pk[b2][:, 0:136]))
                S.emit("vector", lambda e, d=dst, b2=bk6: e.tensor_mul(
                    out=d, in0=GS[:, 1088:1224], in1=pk[b2][:, 136:272]))
            norm_done = S.mark("vector", sV)
            bank_rd[bk6] = (sV, norm_done)
            xb_free[slot] = (sV, norm_done)
            norm_marks.append(norm_done)

        S.emit("vector", lambda e: e.memset(CF[:, 0:NCOL], 0.0).then_inc(sV, 1))
        czero = S.bump(sV)
        S.wait("scalar", sV, czero)
        S.wait("scalar", sXR, consts_done)
        for ft in range(8):
            S.emit("scalar", lambda e, f=ft: e.add(
                out=CF[:, f * 512:(f + 1) * 512], in_=CF[:, f * 512:(f + 1) * 512],
                add=BIA[:, 64 + f: 65 + f]))
        cinit = S.mark("scalar", sS)
        S.wait("vector", sS, cinit)
        S.emit("vector", lambda e: e.tensor_copy(out=CB[:, 0:NCOL], in_=CF[:, 0:NCOL]).then_inc(sV, 1))
        cb_ready = S.bump(sV)

        uslab_done = []
        uslab_free = [(sT, 0), (sT, 0)]

        def emit_uslab(idx):
            slot = idx % 2
            sem, v = uslab_free[slot]
            S.wait("gpsimd", sem, v)
            pair = idx % 12
            src2 = din["UcatT"][:, pair * 256:(pair + 1) * 256]
            src2 = src2.rearrange("(k p) c -> p k c", p=128)
            S.emit("gpsimd", lambda e, s=src2, o=slot * 2048:
                   e.dma_start(out=W2[:, o:o + 2048].rearrange("p (k c) -> p k c", c=256), in_=s).then_inc(sLD, 16))
            uslab_done.append(S.bump(sLD, 16))

        emit_uslab(0)
        emit_uslab(1)
        gfree = (sV, 0)
        ffn_started = False

        def emit_ffn_step(t):
            nonlocal gfree, ffn_started
            if not ffn_started:
                S.wait("tensor", sLD, wcat_done)
                S.wait("tensor", sV, cb_ready)
                ffn_started = True
            S.wait("tensor", sV, norm_marks[4 * t + 3])
            r1 = t * REG + 8
            r2 = t * REG + 280
            for trip in range(8):
                par = trip % 2
                banks = (0, 1, 2) if par == 0 else (3, 4, 5)
                for gi in range(3):
                    oc = 3 * trip + gi
                    bk = banks[gi]
                    sidx = t * 12 + oc // 2
                    slot = sidx % 2
                    t_bank(bk)
                    for k in range(8):
                        lhs = W1[:, k * 3072 + oc * 128: k * 3072 + (oc + 1) * 128]
                        S.emit("tensor", lambda e, l=lhs, f=k, b2=bk, a=r1: e.matmul(
                            pk[b2][:, 0:256], l, XN[:, f * XNW + a: f * XNW + a + 256],
                            start=(f == 0), stop=False))
                        S.emit("tensor", lambda e, l=lhs, f=k, b2=bk, a=r2: e.matmul(
                            pk[b2][:, 256:512], l, XN[:, f * XNW + a: f * XNW + a + 256],
                            start=False, stop=False))
                    S.wait("tensor", sLD, uslab_done[sidx])
                    CBr = CB if t % 2 == 0 else CB1
                    for k in range(8):
                        lhs = W2[:, slot * 2048 + k * 256 + (oc % 2) * 128:
                                 slot * 2048 + k * 256 + (oc % 2) * 128 + 128]
                        S.emit("tensor", lambda e, l=lhs, f=k, b2=bk, cb=CBr: e.matmul(
                            pk[b2][:, 0:512], l, cb[:, f * 512:(f + 1) * 512],
                            start=False, stop=(f == 7)))
                    mm_done = tmark()
                    bank_wr[bk] = (sT, mm_done)
                    if oc % 2 == 1:
                        uslab_free[slot] = (sT, mm_done)
                        if sidx + 2 < ffn_steps * 12:
                            emit_uslab(sidx + 2)
                sem, v = gfree
                S.wait("scalar", sem, v)
                gb = (0, 512, 1024)
                for gi in range(3):
                    bk = banks[gi]
                    r_bank("scalar", bk)
                    fn = AF.Sigmoid if gi < 2 else AF.Tanh
                    oc = 3 * trip + gi
                    S.emit("scalar", lambda e, b2=bk, o=gb[gi], fu=fn, bc=24 + oc: e.activation(
                        out=FS[:, o:o + 512], in_=pk[b2][:, 0:512], func=fu,
                        bias=BIA[:, bc:bc + 1]))
                evac_done = S.mark("scalar", sS)
                for gi in range(3):
                    bank_rd[banks[gi]] = (sS, evac_done)
                S.wait("vector", sS, evac_done)
                csl = slice(trip * 512, (trip + 1) * 512)
                S.emit("vector", lambda e: e.tensor_mul(
                    out=FS[:, 1536:2048], in0=FS[:, 512:1024], in1=FS[:, 1024:1536]))
                S.emit("vector", lambda e, s=csl: e.tensor_mul(
                    out=CF[:, s], in0=FS[:, 0:512], in1=CF[:, s]))
                S.emit("vector", lambda e, s=csl: e.tensor_add(
                    out=CF[:, s], in0=CF[:, s], in1=FS[:, 1536:2048]))
                CBw = CB1 if t % 2 == 0 else CB
                S.emit("vector", lambda e, s=csl, cb=CBw: e.tensor_copy(
                    out=cb[:, s], in_=CF[:, s]).then_inc(sV, 1))
                cup = S.bump(sV)
                gfree = (sV, cup)
            S.wait("tensor", sV, S.val(sV))

        if upto >= 2:
            for q4 in range(4):
                emit_ln_chunk(q4)
            for t in range(8):
                if t < ffn_steps:
                    emit_ffn_step(t)
                if t < 7:
                    for q4 in range(4):
                        emit_ln_chunk(4 * (t + 1) + q4)
        else:
            for ci in range(32):
                emit_ln_chunk(ci)
        ln_done_V = S.val(sV)

        if "xn" in tap_h:
            S.wait("sync", sV, ln_done_V)
            S.emit("sync", lambda e: e.dma_start(out=tap_h["xn"], in_=XN[:, 0:8 * XNW]).then_inc(sDO, 16))
            S.bump(sDO, 16)

        if upto >= 2:
            uoslab_done = [None] * 4
            uoslab_free = [uslab_free[0], uslab_free[1]]

            def emit_uoslab(idx):
                slot = idx % 2
                sem, v = uoslab_free[slot]
                S.wait("gpsimd", sem, v)
                src2 = din["uoT"][:, idx * 256:(idx + 1) * 256]
                src2 = src2.rearrange("(k p) c -> p k c", p=128)
                S.emit("gpsimd", lambda e, s=src2, o=slot * 2048:
                       e.dma_start(out=W2[:, o:o + 2048].rearrange("p (k c) -> p k c", c=256), in_=s).then_inc(sLD, 16))
                uoslab_done[idx] = S.bump(sLD, 16)

            emit_uoslab(0)
            emit_uoslab(1)
            for ft in range(8):
                bk = 6 + (ft % 2)
                pair = ft // 2
                slot = pair % 2
                t_bank(bk)
                S.wait("tensor", sLD, uoslab_done[pair])
                for k in range(8):
                    lhs = W2[:, slot * 2048 + k * 256 + (ft % 2) * 128:
                             slot * 2048 + k * 256 + (ft % 2) * 128 + 128]
                    S.emit("tensor", lambda e, l=lhs, f=k, b2=bk: e.matmul(
                        pk[b2][:, 0:512], l, CB[:, f * 512:(f + 1) * 512],
                        start=(f == 0), stop=(f == 7)))
                mm_done = tmark()
                bank_wr[bk] = (sT, mm_done)
                if ft % 2 == 1:
                    uoslab_free[slot] = (sT, mm_done)
                    if pair + 2 < 4:
                        emit_uoslab(pair + 2)
                S.wait("scalar", sT, mm_done)
                S.emit("scalar", lambda e, f=ft, b2=bk: e.activation(
                    out=CUO[:, f * 512:(f + 1) * 512], in_=pk[b2][:, 0:512],
                    func=AF.Identity, bias=BIA[:, 48 + f:49 + f]).then_inc(sS, 1))
                cuo_done = S.bump(sS)
                bank_rd[bk] = (sS, cuo_done)
            cuo_S = S.val(sS)

            if "cb1" in tap_h:
                S.wait("sync", sV, S.val(sV))
                S.emit("sync", lambda e: e.dma_start(out=tap_h["cb1"], in_=CB1[:, 0:NCOL]).then_inc(sDO, 16))
                S.bump(sDO, 16)
            if "cb" in tap_h:
                cbfin = CB if ffn_steps % 2 == 0 else CB1
                S.wait("sync", sV, S.val(sV))
                S.emit("sync", lambda e, cb=cbfin: e.dma_start(out=tap_h["cb"], in_=cb[:, 0:NCOL]).then_inc(sDO, 16))
                S.bump(sDO, 16)
            if "cuo" in tap_h:
                S.wait("sync", sS, cuo_S)
                S.emit("sync", lambda e: e.dma_start(out=tap_h["cuo"], in_=CUO[:, 0:NCOL]).then_inc(sDO, 16))
                S.bump(sDO, 16)

        if upto >= 3:
            ffn_T = S.val(sT)
            ffn_V = S.val(sV)
            cf_free = ffn_V
            out_guard = [0, 0]
            S.wait("vector", sT, ffn_T)
            for ft in range(8):
                for p in range(2):
                    S.emit("vector", lambda e, f=ft, q=p: e.memset(
                        W1[:, KH0 + f * 768 + q * 384 + 272: KH0 + f * 768 + (q + 1) * 384], 0.0))
            for tau in range(6):
                S.emit("vector", lambda e, u=tau: e.memset(
                    W1[:, VB0 + u * 1040 + 64: VB0 + u * 1040 + 65 * 16: 65], 1.0))
            p3_init = S.mark("vector", sV)

            qslab_done = [None] * (NB * 12)
            qslab_free = [(sT, ffn_T), (sT, ffn_T)]

            def emit_qslab(idx):
                slot = idx % 2
                sem, v = qslab_free[slot]
                S.wait("gpsimd", sem, v)
                pair = idx % 12
                src2 = din["wqkvT"][:, pair * 256:(pair + 1) * 256]
                src2 = src2.rearrange("(k p) c -> p k c", p=128)
                S.emit("gpsimd", lambda e, s=src2, o=SLAB0 + slot * 2048:
                       e.dma_start(out=W1[:, o:o + 2048].rearrange("p (k c) -> p k c", c=256), in_=s).then_inc(sLD, 16))
                qslab_done[idx] = S.bump(sLD, 16)

            def qslab_consumed(idx, mark):
                qslab_free[idx % 2] = (sT, mark)
                if idx + 2 < NB * 12:
                    emit_qslab(idx + 2)

            fslab_done = [None] * (NB * 8)
            fslab_free = [(sT, ffn_T), (sT, ffn_T)]

            def emit_fslab(idx):
                slot = idx % 2
                sem, v = fslab_free[slot]
                S.wait("gpsimd", sem, v)
                i = idx % 8
                which = "woutT" if i % 2 == 0 else "woT"
                pair = i // 2
                src2 = din[which][:, pair * 256:(pair + 1) * 256]
                src2 = src2.rearrange("(k p) c -> p k c", p=128)
                S.emit("gpsimd", lambda e, s=src2, o=slot * 2048:
                       e.dma_start(out=W2[:, o:o + 2048].rearrange("p (k c) -> p k c", c=256), in_=s).then_inc(sLD, 16))
                fslab_done[idx] = S.bump(sLD, 16)

            def fslab_consumed(idx, mark):
                fslab_free[idx % 2] = (sT, mark)
                if idx + 2 < NB * 8:
                    emit_fslab(idx + 2)

            emit_qslab(0)
            emit_qslab(1)
            if upto >= 5:
                emit_fslab(0)
                emit_fslab(1)

            for b in range(p3_b):
                for oc in range(16):
                    pair = oc // 2
                    idx = b * 12 + pair
                    slot = idx % 2
                    bkM = 6 + (oc % 2)
                    bkS = 4 + (oc % 2)
                    t_bank(bkM); t_bank(bkS)
                    S.wait("tensor", sLD, qslab_done[idx])
                    if b == 0 and oc == 0:
                        S.wait("tensor", sV, p3_init)
                    for k in range(8):
                        lhs = W1[:, SLAB0 + slot * 2048 + k * 256 + (oc % 2) * 128:
                                 SLAB0 + slot * 2048 + k * 256 + (oc % 2) * 128 + 128]
                        S.emit("tensor", lambda e, l=lhs, f=k, bb=b, b2=bkM: e.matmul(
                            pk[b2][:, 0:512], l,
                            XN[:, f * XNW + bb * REG: f * XNW + bb * REG + 512],
                            start=(f == 0), stop=(f == 7)))
                        S.emit("tensor", lambda e, l=lhs, f=k, bb=b, b2=bkS: e.matmul(
                            pk[b2][:, 0:32], l,
                            XN[:, f * XNW + bb * REG + 512: f * XNW + bb * REG + 544],
                            start=(f == 0), stop=(f == 7)))
                    mm_done = tmark()
                    bank_wr[bkM] = (sT, mm_done)
                    bank_wr[bkS] = (sT, mm_done)
                    if oc % 2 == 1:
                        qslab_consumed(idx, mm_done)
                    ft = oc % 8
                    r_bank("scalar", bkM); r_bank("scalar", bkS)
                    if oc < 8:
                        bcol = ft
                        d0 = QH0 + ft * 512
                        S.emit("scalar", lambda e, b2=bkM, d=d0, bc=bcol: e.activation(
                            out=W1[:, d:d + 256], in_=pk[b2][:, 8:264], func=AF.Identity,
                            bias=BIA[:, bc:bc + 1]))
                        S.emit("scalar", lambda e, b2=bkM, d=d0, bc=bcol: e.activation(
                            out=W1[:, d + 256:d + 488], in_=pk[b2][:, 280:512], func=AF.Identity,
                            bias=BIA[:, bc:bc + 1]))
                        S.emit("scalar", lambda e, b2=bkS, d=d0, bc=bcol: e.activation(
                            out=W1[:, d + 488:d + 512], in_=pk[b2][:, 0:24], func=AF.Identity,
                            bias=BIA[:, bc:bc + 1]).then_inc(sS, 1))
                    else:
                        bcol = 8 + ft
                        d0 = KH0 + ft * 768
                        S.emit("scalar", lambda e, b2=bkM, d=d0, bc=bcol: e.activation(
                            out=W1[:, d:d + 272], in_=pk[b2][:, 0:272], func=AF.Identity,
                            bias=BIA[:, bc:bc + 1]))
                        S.emit("scalar", lambda e, b2=bkM, d=d0, bc=bcol: e.activation(
                            out=W1[:, d + 384:d + 624], in_=pk[b2][:, 272:512], func=AF.Identity,
                            bias=BIA[:, bc:bc + 1]))
                        S.emit("scalar", lambda e, b2=bkS, d=d0, bc=bcol: e.activation(
                            out=W1[:, d + 624:d + 656], in_=pk[b2][:, 0:32], func=AF.Identity,
                            bias=BIA[:, bc:bc + 1]).then_inc(sS, 1))
                    ev_done = S.bump(sS)
                    bank_rd[bkM] = (sS, ev_done)
                    bank_rd[bkS] = (sS, ev_done)
                qk_evac_S = S.val(sS)
                if p3_parts < 2:
                    continue

                for vp in range(4):
                    idx = b * 12 + 8 + vp
                    slot = idx % 2
                    S.wait("tensor", sLD, qslab_done[idx])
                    if b == 0 and vp == 0:
                        S.wait("tensor", sLD, bvb_done)
                    for tau in range(6):
                        bk = tau // 2
                        t_bank(bk)
                        p, tt2 = tau // 3, tau % 3
                        win = b * REG + p * 272 + 128 * tt2
                        co = (tau % 2) * 256
                        for k in range(8):
                            S.emit("tensor", lambda e, f=k, w=win, b2=bk, c2=co, sl=slot: e.matmul(
                                pk[b2][:, c2:c2 + 256],
                                XN[:, f * XNW + w: f * XNW + w + 128],
                                W1[:, SLAB0 + sl * 2048 + f * 256: SLAB0 + sl * 2048 + (f + 1) * 256],
                                start=(f == 0), stop=False))
                        S.emit("tensor", lambda e, b2=bk, c2=co, v2=vp: e.matmul(
                            pk[b2][:, c2:c2 + 256], ONE[0:1, 0:128],
                            BVB[0:1, v2 * 256:(v2 + 1) * 256], start=False, stop=True))
                    mm_done = tmark()
                    for bk in range(3):
                        bank_wr[bk] = (sT, mm_done)
                    qslab_consumed(idx, mm_done)
                    for tau in range(6):
                        bk = tau // 2
                        r_bank("vector", bk)
                        co = (tau % 2) * 256
                        dst = W1[:, VB0 + tau * 1040 + 65 * (4 * vp):
                                 VB0 + tau * 1040 + 65 * (4 * vp) + 260]
                        dst = dst.rearrange("q (h d) -> q h d", h=4, d=65)[:, :, 0:64]
                        src2 = pk[bk][:, co:co + 256].rearrange("q (h d) -> q h d", h=4, d=64)
                        S.emit("vector", lambda e, d=dst, s=src2: e.tensor_copy(out=d, in_=s))
                    vev = S.mark("vector", sV)
                    for bk in range(3):
                        bank_rd[bk] = (sV, vev)
                vb_done_V = S.val(sV)
                if p3_parts < 3:
                    continue

                t_bank(6)
                S.wait("vector", sS, qk_evac_S)
                sq_mark = [0, 0]
                for ft in range(8):
                    sl = ft % 2
                    if sq_mark[sl]:
                        S.wait("vector", sT, sq_mark[sl])
                    S.emit("vector", lambda e, f=ft, o=sl * 512: e.tensor_mul(
                        out=GS[:, o:o + 512], in0=W1[:, QH0 + f * 512: QH0 + (f + 1) * 512],
                        in1=W1[:, QH0 + f * 512: QH0 + (f + 1) * 512]).then_inc(sV, 1))
                    sqd = S.bump(sV)
                    S.wait("tensor", sV, sqd)
                    S.emit("tensor", lambda e, f=ft, o=sl * 512: e.matmul(
                        pk[6][0:16, 0:512], SEL2[:, f * 16:(f + 1) * 16], GS[:, o:o + 512],
                        start=(f == 0), stop=(f == 7)))
                    sq_mark[sl] = tmark()
                bank_wr[6] = (sT, sq_mark[7 % 2])
                r_bank("scalar", 6)
                S.emit("scalar", lambda e: e.activation(
                    out=RN[0:16, 0:512], in_=pk[6][0:16, 0:512], func=AF.Ln, bias=MRK[0:16, 3:4]))
                S.emit("scalar", lambda e: e.activation(
                    out=RN[0:16, 0:512], in_=RN[0:16, 0:512], func=AF.Exp, scale=0.5).then_inc(sS, 1))
                s_done = S.bump(sS)
                bank_rd[6] = (sS, s_done)
                S.wait("vector", sS, s_done)
                S.emit("vector", lambda e: e.tensor_scalar_add(
                    out=RN[0:16, 0:512], in0=RN[0:16, 0:512], scalar1=1e-6))
                S.emit("vector", lambda e: e.reciprocal(
                    out=RN[0:16, 0:512], in_=RN[0:16, 0:512]))
                S.emit("vector", lambda e: e.tensor_copy(
                    out=RNB[0:16, 0:512], in_=RN[0:16, 0:512]).then_inc(sV, 1))
                rnq_done = S.bump(sV)
                for ft in range(8):
                    bb7 = 6 + (ft % 2)
                    t_bank(bb7)
                    if ft == 0:
                        S.wait("tensor", sV, rnq_done)
                    S.emit("tensor", lambda e, f=ft, b2=bb7: e.matmul(
                        pk[b2][:, 0:512], SEL[0:64, f * 128:(f + 1) * 128], RNB[0:64, 0:512],
                        start=True, stop=True))
                    bc = tmark()
                    bank_wr[bb7] = (sT, bc)
                    S.wait("vector", sT, bc)
                    S.emit("vector", lambda e, f=ft, b2=bb7: e.tensor_mul(
                        out=W1[:, QH0 + f * 512: QH0 + (f + 1) * 512],
                        in0=W1[:, QH0 + f * 512: QH0 + (f + 1) * 512],
                        in1=pk[b2][:, 0:512]).then_inc(sV, 1))
                    qn = S.bump(sV)
                    bank_rd[bb7] = (sV, qn)

                if p3_parts < 4:
                    continue
                for p in range(2):
                    t_bank(4 + p)
                    sq_mark = [0, 0]
                    for ft in range(8):
                        sl = ft % 2
                        if sq_mark[sl]:
                            S.wait("vector", sT, sq_mark[sl])
                        S.emit("vector", lambda e, f=ft, q=p, o=1024 + sl * 384: e.tensor_mul(
                            out=GS[:, o:o + 384],
                            in0=W1[:, KH0 + f * 768 + q * 384: KH0 + f * 768 + q * 384 + 384],
                            in1=W1[:, KH0 + f * 768 + q * 384: KH0 + f * 768 + q * 384 + 384]).then_inc(sV, 1))
                        ksd = S.bump(sV)
                        S.wait("tensor", sV, ksd)
                        S.emit("tensor", lambda e, f=ft, b2=4 + p, o=1024 + sl * 384: e.matmul(
                            pk[b2][0:16, 0:384], SEL2[:, f * 16:(f + 1) * 16], GS[:, o:o + 384],
                            start=(f == 0), stop=(f == 7)))
                        sq_mark[sl] = tmark()
                    bank_wr[4 + p] = (sT, S.val(sT))
                    r_bank("scalar", 4 + p)
                    S.emit("scalar", lambda e, b2=4 + p: e.activation(
                        out=RN[0:16, 0:384], in_=pk[b2][0:16, 0:384], func=AF.Ln, bias=MRK[0:16, 3:4]))
                    S.emit("scalar", lambda e: e.activation(
                        out=RN[0:16, 0:384], in_=RN[0:16, 0:384], func=AF.Exp, scale=0.5).then_inc(sS, 1))
                    ksd2 = S.bump(sS)
                    bank_rd[4 + p] = (sS, ksd2)
                    S.wait("vector", sS, ksd2)
                    S.emit("vector", lambda e: e.tensor_scalar_add(
                        out=RN[0:16, 0:384], in0=RN[0:16, 0:384], scalar1=1e-6))
                    S.emit("vector", lambda e: e.reciprocal(
                        out=RN[0:16, 0:384], in_=RN[0:16, 0:384]))
                    S.emit("vector", lambda e: e.tensor_copy(
                        out=RNB[0:16, 0:384], in_=RN[0:16, 0:384]).then_inc(sV, 1))
                    rnk_done = S.bump(sV)
                    for ft in range(8):
                        bb7 = 6 + (ft % 2)
                        t_bank(bb7)
                        if ft == 0:
                            S.wait("tensor", sV, rnk_done)
                        S.emit("tensor", lambda e, f=ft, b2=bb7: e.matmul(
                            pk[b2][:, 0:384], SEL[0:64, f * 128:(f + 1) * 128], RNB[0:64, 0:384],
                            start=True, stop=True))
                        bc = tmark()
                        bank_wr[bb7] = (sT, bc)
                        S.wait("vector", sT, bc)
                        S.emit("vector", lambda e, f=ft, q=p, b2=bb7: e.tensor_mul(
                            out=W1[:, KH0 + f * 768 + q * 384: KH0 + f * 768 + q * 384 + 384],
                            in0=W1[:, KH0 + f * 768 + q * 384: KH0 + f * 768 + q * 384 + 384],
                            in1=pk[b2][:, 0:384]).then_inc(sV, 1))
                        kn = S.bump(sV)
                        bank_rd[bb7] = (sV, kn)
                norm_done_V = S.val(sV)

                if p3_parts < 5:
                    continue
                units = [(p, qb, half) for p in range(2) for qb in range(2) for half in range(2)]
                pbbase = [None, None]
                exp_marks = {}
                mask_marks = {}
                avm_marks = {}
                evac_marks = {}
                rzc_marks = {}
                ttn_marks = {}
                zsl = [ZR[0:1, 0:512], ZR[0:1, 512:1024], ROWB[0:1, 0:512], ROWB[0:1, 512:1024]]
                rzsl = [GS[:, 0:512], GS[:, 512:1024]]

                def pb_ap(u, bk, cols=512, co=0):
                    if u % 2 == 0:
                        return W1[:, PB0 + bk * 512 + co: PB0 + bk * 512 + co + cols]
                    return LNB[:, bk * 512 + co: bk * 512 + co + cols]

                for u in range(10):
                    if u < 8:
                        p, qb, half = units[u]
                        for i in range(4):
                            t_bank(i)
                        if u == 0 and b == 0:
                            pass
                        if u == 0:
                            S.wait("tensor", sV, norm_done_V)
                            S.wait("tensor", sV, vb_done_V)
                        for hl in range(8):
                            h = half * 8 + hl
                            fth, po = h // 2, (h % 2) * 64
                            q0 = QH0 + fth * 512 + p * 256 + qb * 128
                            for ab in range(2):
                                tt2 = qb + ab
                                k0 = KH0 + fth * 768 + p * 384 + 128 * tt2
                                bk = ab * 2 + (hl % 2)
                                S.emit("tensor", lambda e, kk=k0, qq=q0, b2=bk,
                                       c2=(hl // 2) * 128, o=po: e.matmul(
                                    pk[b2][:, c2:c2 + 128],
                                    W1[o:o + 64, kk:kk + 128],
                                    W1[o:o + 64, qq:qq + 128],
                                    start=True, stop=True))
                        smm = tmark()
                        for i in range(4):
                            bank_wr[i] = (sT, smm)
                        if u - 2 >= 0:
                            S.wait("scalar", sT, avm_marks[u - 2])
                        for i in range(4):
                            r_bank("scalar", i)
                            S.emit("scalar", lambda e, b2=i, uu=u: e.activation(
                                out=pb_ap(uu, b2), in_=pk[b2][:, 0:512], func=AF.Exp))
                        exp_marks[u] = S.mark("scalar", sS)
                        for i in range(4):
                            bank_rd[i] = (sS, exp_marks[u])
                        S.wait("vector", sS, exp_marks[u])
                        for i in range(4):
                            ab = i // 2
                            mc = (qb * 2 + ab) * 128
                            for j in range(4):
                                S.emit("vector", lambda e, b2=i, jj=j, m=mc, uu=u: e.tensor_mul(
                                    out=pb_ap(uu, b2, 128, jj * 128),
                                    in0=pb_ap(uu, b2, 128, jj * 128),
                                    in1=MSK[:, m:m + 128]))
                        mask_marks[u] = S.mark("vector", sV)
                    if 1 <= u < 9:
                        v = u - 1
                        p, qb, half = units[v]
                        t_bank(4); t_bank(5)
                        S.wait("tensor", sV, mask_marks[v])
                        for hl in range(8):
                            h = half * 8 + hl
                            ub = 4 + hl // 4
                            uc_ = (hl % 4) * 128
                            for ab in range(2):
                                tau = p * 3 + qb + ab
                                v0 = VB0 + tau * 1040 + 65 * h
                                pbk = ab * 2 + (hl % 2)
                                S.emit("tensor", lambda e, vv=v0, b2=ub, c2=uc_, uu=v,
                                       p2=pbk, pc=(hl // 2) * 128, aa=ab: e.matmul(
                                    pk[b2][0:65, c2:c2 + 128], W1[:, vv:vv + 65],
                                    pb_ap(uu, p2, 128, pc), start=(aa == 0), stop=(aa == 1)))
                        avm_marks[v] = tmark()
                        bank_wr[4] = (sT, avm_marks[v])
                        bank_wr[5] = (sT, avm_marks[v])
                        for ui in range(2):
                            ub = 4 + ui
                            r_bank("vector", ub)
                            for j in range(4):
                                h = half * 8 + ui * 4 + j
                                fth, po = h // 2, (h % 2) * 64
                                dst = CB1[po:po + 64,
                                          fth * 512 + p * 256 + qb * 128:
                                          fth * 512 + p * 256 + qb * 128 + 128]
                                S.emit("vector", lambda e, d=dst, b2=ub, c2=j * 128: e.tensor_copy(
                                    out=d, in_=pk[b2][0:64, c2:c2 + 128]))
                            S.emit("vector", lambda e, b2=ub, z=zsl[(v % 2) * 2 + ui]: e.reciprocal(
                                out=z, in_=pk[b2][64:65, 0:512]))
                        evac_marks[v] = S.mark("vector", sV)
                        bank_rd[4] = (sV, evac_marks[v])
                        bank_rd[5] = (sV, evac_marks[v])
                    if 2 <= u:
                        w = u - 2
                        if w < 8:
                            p, qb, half = units[w]
                            for ui in range(2):
                                bkz = 6 + ui
                                t_bank(bkz)
                                if ui == 0:
                                    S.wait("tensor", sV, evac_marks[w])
                                S.emit("tensor", lambda e, b2=bkz, z=zsl[(w % 2) * 2 + ui]: e.matmul(
                                    pk[b2][:, 0:512], ONE[0:1, 0:128], z, start=True, stop=True))
                                bcm = tmark()
                                bank_wr[bkz] = (sT, bcm)
                                S.wait("scalar", sT, bcm)
                                if w - 1 >= 0 and (w - 1) in ttn_marks:
                                    S.wait("scalar", sV, ttn_marks[w - 1])
                                S.emit("scalar", lambda e, b2=bkz, r=rzsl[ui]: e.copy(
                                    out=r, in_=pk[b2][:, 0:512]).then_inc(sS, 1))
                                rzc = S.bump(sS)
                                rzc_marks[(w, ui)] = rzc
                                bank_rd[bkz] = (sS, rzc)
                            S.wait("vector", sS, rzc_marks[(w, 1)])
                            for ui in range(2):
                                for j in range(4):
                                    h = half * 8 + ui * 4 + j
                                    fth, po = h // 2, (h % 2) * 64
                                    dst = CB1[po:po + 64,
                                              fth * 512 + p * 256 + qb * 128:
                                              fth * 512 + p * 256 + qb * 128 + 128]
                                    S.emit("vector", lambda e, d=dst, r=rzsl[ui], o=po,
                                           c2=j * 128: e.tensor_mul(
                                        out=d, in0=d, in1=r[o:o + 64, c2:c2 + 128]))
                            ttn_marks[w] = S.mark("vector", sV)
                attn_done_V = S.val(sV)

                if upto >= 5:
                    S.wait("scalar", sV, cf_free)
                    srcx = din["xr"][:, b * 512:(b + 1) * 512].rearrange("(ft p) c -> p ft c", p=128)
                    S.emit("scalar", lambda e, s=srcx: e.dma_start(
                        out=CF[:, 0:NCOL].rearrange("p (ft c) -> p ft c", c=512),
                        in_=s).then_inc(sXR, 16))
                    xr_done = S.bump(sXR, 16)
                    for ft in range(8):
                        fi = b * 8 + ft
                        iA = b * 8 + 2 * (ft // 2)
                        iB = iA + 1
                        bkA = 0 + (ft % 2)
                        bkB = 2 + (ft % 2)
                        t_bank(bkA); t_bank(bkB)
                        if ft == 0:
                            S.wait("tensor", sV, attn_done_V)
                        S.wait("tensor", sLD, fslab_done[iA])
                        for k in range(8):
                            lhs = W2[:, (iA % 2) * 2048 + k * 256 + (ft % 2) * 128:
                                     (iA % 2) * 2048 + k * 256 + (ft % 2) * 128 + 128]
                            S.emit("tensor", lambda e, l=lhs, f=k, b2=bkA: e.matmul(
                                pk[b2][:, 0:512], l, CB1[:, f * 512:(f + 1) * 512],
                                start=(f == 0), stop=(f == 7)))
                        S.wait("tensor", sLD, fslab_done[iB])
                        for k in range(8):
                            lhs = W2[:, (iB % 2) * 2048 + k * 256 + (ft % 2) * 128:
                                     (iB % 2) * 2048 + k * 256 + (ft % 2) * 128 + 128]
                            S.emit("tensor", lambda e, l=lhs, f=k, bb=b, b2=bkB: e.matmul(
                                pk[b2][:, 0:256], l,
                                XN[:, f * XNW + bb * REG + 8: f * XNW + bb * REG + 264],
                                start=(f == 0), stop=False))
                            S.emit("tensor", lambda e, l=lhs, f=k, bb=b, b2=bkB: e.matmul(
                                pk[b2][:, 256:512], l,
                                XN[:, f * XNW + bb * REG + 280: f * XNW + bb * REG + 536],
                                start=False, stop=(f == 7)))
                        pm = tmark()
                        bank_wr[bkA] = (sT, pm)
                        bank_wr[bkB] = (sT, pm)
                        if ft % 2 == 1:
                            fslab_consumed(iA, pm)
                            fslab_consumed(iB, pm)
                        r_bank("vector", bkB)
                        S.emit("vector", lambda e, b2=bkB, f=ft: e.tensor_add(
                            out=FS[:, 1024:1536], in0=pk[b2][:, 0:512],
                            in1=CUO[:, f * 512:(f + 1) * 512]).then_inc(sV, 1))
                        e1 = S.bump(sV)
                        bank_rd[bkB] = (sV, e1)
                        S.wait("scalar", sV, e1)
                        S.emit("scalar", lambda e: e.activation(
                            out=FS[:, 1536:2048], in_=FS[:, 1024:1536], func=AF.Tanh).then_inc(sS, 1))
                        e2 = S.bump(sS)
                        r_bank("vector", bkA)
                        S.wait("vector", sS, e2)
                        S.emit("vector", lambda e, b2=bkA: e.tensor_add(
                            out=FS[:, 1024:1536], in0=pk[b2][:, 0:512], in1=FS[:, 1536:2048]).then_inc(sV, 1))
                        e3 = S.bump(sV)
                        bank_rd[bkA] = (sV, e3)
                        gm = out_guard[fi % 2]
                        if gm:
                            S.wait("vector", sDO, gm)
                        if ft == 0:
                            S.wait("vector", sXR, xr_done)
                        S.emit("vector", lambda e, f=ft, oo=2048 + (fi % 2) * 512:
                               e.scalar_tensor_tensor(
                                   out=FS[:, oo:oo + 512], in0=FS[:, 1024:1536],
                                   scalar=BIA[:, 56 + f:57 + f], in1=CF[:, f * 512:(f + 1) * 512],
                                   op0=ALU.add, op1=ALU.add).then_inc(sV, 1))
                        e4 = S.bump(sV)
                        if ft == 7:
                            cf_free = e4
                        S.wait("sync", sV, e4)
                        S.emit("sync", lambda e, f=ft, bb=b, oo=2048 + (fi % 2) * 512: e.dma_start(
                            out=out_h[f * 128:(f + 1) * 128, bb * 512:(bb + 1) * 512],
                            in_=FS[:, oo:oo + 512]).then_inc(sDO, 16))
                        out_guard[fi % 2] = S.bump(sDO, 16)

                if upto >= 4 and "ap" in tap_h and b == p3_b - 1:
                    S.wait("sync", sV, attn_done_V)
                    S.emit("sync", lambda e: e.dma_start(out=tap_h["ap"], in_=CB1[:, 0:NCOL]).then_inc(sDO, 16))
                    S.bump(sDO, 16)
            if upto >= 3 and "qh" in tap_h:
                S.wait("sync", sV, S.val(sV))
                S.emit("sync", lambda e: e.dma_start(out=tap_h["qh"], in_=W1[:, QH0:QH0 + 4096]).then_inc(sDO, 16))
                S.bump(sDO, 16)
                S.emit("sync", lambda e: e.dma_start(out=tap_h["kh"], in_=W1[:, KH0:KH0 + 6144]).then_inc(sDO, 16))
                S.bump(sDO, 16)
                S.emit("sync", lambda e: e.dma_start(out=tap_h["vb"], in_=W1[:, VB0:VB0 + 6240]).then_inc(sDO, 16))
                S.bump(sDO, 16)

        stk.enter_context(nc.allow_low_precision(
            reason="bf16 matmul operands with fp32 PSUM accumulation; fp32 elementwise"))
        with nc.Block() as block:
            @block.tensor
            def _(e):
                for f in S.q["tensor"]:
                    f(e)

            @block.vector
            def _(e):
                for f in S.q["vector"]:
                    f(e)

            @block.scalar
            def _(e):
                for f in S.q["scalar"]:
                    f(e)

            @block.gpsimd
            def _(e):
                for f in S.q["gpsimd"]:
                    f(e)

            @block.sync
            def _(e):
                for f in S.q["sync"]:
                    f(e)
                e.wait_ge(sDO, S.val(sDO))
    return nc


def kernel(**inputs):
    per_core = _host_prep(inputs)
    nc = _build()
    res = run_bass_kernel_spmd(nc, [dict(pc) for pc in per_core], core_ids=list(range(NCORE)))
    return _host_assemble([r["out"] for r in res.results])
